# version 01
# baseline (speedup 1.0000x reference)
import math
import numpy as np
import ml_dtypes

import concourse.bass as bass
import concourse.mybir as mybir
from concourse.bass_utils import run_bass_kernel_spmd
from contextlib import ExitStack

F32 = mybir.dt.float32
BF16 = mybir.dt.bfloat16
I32 = mybir.dt.int32
I8 = mybir.dt.int8
AF = mybir.ActivationFunctionType
ALU = mybir.AluOpType
AX = mybir.AxisListType

NC = 8
S = 8192
H = 4096
HD = 128
ROWS = S // NC
FC = H // NC
SCALE = HD ** -0.5
M_STAB = 12.0
NEG = -1.0e9

L0_SLOTS = [(64, 768), (256, 2560), (512, None), (512, None)]
L1_SLOTS = [(512, None)] * 4
_ATTN_TRUNC = None
_PROJ_TRUNC = None

BF = ml_dtypes.bfloat16


def _alibi_slopes(n):
    cp2 = 2 ** math.floor(math.log2(n))
    base = 2 ** (-(2 ** -(math.log2(cp2) - 3)))
    slopes = np.power(base, np.arange(1, 1 + cp2, dtype=np.float64))
    if cp2 != n:
        eb = 2 ** (-(2 ** -(math.log2(2 * cp2) - 3)))
        rem = min(cp2, n - cp2)
        slopes = np.concatenate(
            [slopes, np.power(eb, np.arange(1, 1 + 2 * rem, 2, dtype=np.float64))])
    return slopes.astype(np.float32)


def _sems(nc, st, names):
    return {n: st.enter_context(nc.semaphore(n)) for n in names}



def build_ln():
    nc = bass.Bass("TRN2", target_bir_lowering=False)
    x = nc.dram_tensor("x", [ROWS, H], F32, kind="ExternalInput").ap()
    epscol = nc.dram_tensor("epscol", [128, 1], F32, kind="ExternalInput").ap()
    xn = nc.dram_tensor("xn", [ROWS, H], BF16, kind="ExternalOutput").ap()
    T = ROWS // 128
    with ExitStack() as st:
        sb = lambda n, s_, d: st.enter_context(nc.sbuf_tensor(n, s_, d))
        xb = [sb(f"xb{i}", [128, H], F32) for i in range(2)]
        sq = [sb(f"sq{i}", [128, H], F32) for i in range(2)]
        xnb = [sb(f"xnb{i}", [128, H], BF16) for i in range(2)]
        ssum = sb("ssum", [128, T], F32)
        ssq = sb("ssq", [128, T], F32)
        meanv = sb("meanv", [128, T], F32)
        ex2 = sb("ex2", [128, T], F32)
        negm2 = sb("negm2", [128, T], F32)
        varv = sb("varv", [128, T], F32)
        sqv = sb("sqv", [128, T], F32)
        rstd = sb("rstd", [128, T], F32)
        negmr = sb("negmr", [128, T], F32)
        eps_sb = sb("eps_sb", [128, 1], F32)
        lnv = sb("lnv", [128, T], F32)
        s = _sems(nc, st, ["x0", "x1", "sq", "var", "sqrt", "rn", "xn", "done0",
                           "done1", "eps", "ln", "vch"])
        x_p = [s["x0"], s["x1"]]
        done_p = [s["done0"], s["done1"]]
        block = st.enter_context(nc.Block())

        @block.sync
        def _(sync):
            sync.dma_start(eps_sb[:], epscol).then_inc(s["eps"], 16)
            for t in range(T):
                if t >= 2:
                    sync.wait_ge(s["xn"], t - 1)
                sync.dma_start(xb[t % 2][:], x[t * 128:(t + 1) * 128, :]
                               ).then_inc(x_p[t % 2], 16)

        @block.vector
        def _(v):
            nch = [0]
            def edge(ins):
                nch[0] += 1
                ins.then_inc(s["vch"], 1)
                v.wait_ge(s["vch"], nch[0])
            for t in range(T):
                v.wait_ge(x_p[t % 2], 16 * (t // 2 + 1))
                edge(v.reduce_sum(ssum[:, t:t + 1], xb[t % 2][:], axis=AX.X))
                v.wait_ge(s["sq"], t + 1)
                v.tensor_scalar(meanv[:, t:t + 1], ssum[:, t:t + 1], 1.0 / H, None, ALU.mult)
                edge(v.tensor_scalar(ex2[:, t:t + 1], ssq[:, t:t + 1], 1.0 / H, None,
                                     ALU.mult))
                edge(v.scalar_tensor_tensor(negm2[:, t:t + 1], meanv[:, t:t + 1], -1.0,
                                            meanv[:, t:t + 1], ALU.mult, ALU.mult))
                edge(v.tensor_tensor(varv[:, t:t + 1], ex2[:, t:t + 1], negm2[:, t:t + 1],
                                     ALU.add))
                if t == 0:
                    v.wait_ge(s["eps"], 16)
                v.tensor_scalar(varv[:, t:t + 1], varv[:, t:t + 1], eps_sb[:],
                                None, ALU.add).then_inc(s["var"], 1)
                v.wait_ge(s["sqrt"], t + 1)
                edge(v.tensor_copy(rstd[:, t:t + 1], sqv[:, t:t + 1]))
                edge(v.scalar_tensor_tensor(negmr[:, t:t + 1], meanv[:, t:t + 1], -1.0,
                                            rstd[:, t:t + 1], ALU.mult, ALU.mult))
                if t >= 2:
                    v.wait_ge(done_p[t % 2], 16 * (t // 2))
                v.tensor_scalar(xnb[t % 2][:], xb[t % 2][:], rstd[:, t:t + 1],
                                negmr[:, t:t + 1], ALU.mult, ALU.add
                                ).then_inc(s["xn"], 1)

        @block.scalar
        def _(sc):
            for t in range(T):
                sc.wait_ge(x_p[t % 2], 16 * (t // 2 + 1))
                sc.activation(sq[t % 2][:], xb[t % 2][:], AF.Square,
                              accum_out=ssq[:, t:t + 1]).then_inc(s["sq"], 1)
                sc.wait_ge(s["var"], t + 1)
                sc.activation(lnv[:, t:t + 1], varv[:, t:t + 1], AF.Ln
                              ).then_inc(s["ln"], 1)
                sc.wait_ge(s["ln"], t + 1)
                sc.activation(sqv[:, t:t + 1], lnv[:, t:t + 1], AF.Exp,
                              scale=-0.5).then_inc(s["sqrt"], 1)


        @block.gpsimd
        def _(g):
            for t in range(T):
                g.wait_ge(s["xn"], t + 1)
                g.dma_start(xn[t * 128:(t + 1) * 128, :], xnb[t % 2][:]
                            ).then_inc(done_p[t % 2], 16)
            g.wait_ge(done_p[0], 16 * (T // 2))
            g.wait_ge(done_p[1], 16 * (T // 2))
    return nc



def build_proj(layer):
    nc = bass.Bass("TRN2", target_bir_lowering=False)
    xnT = nc.dram_tensor("xnT", [H, S], BF16, kind="ExternalInput").ap()
    ident_in = nc.dram_tensor("ident", [128, 128], BF16, kind="ExternalInput").ap()
    if layer == 0:
        WB = 1536
        w_all = nc.dram_tensor("w_all", [H, 1536], F32, kind="ExternalInput").ap()
        b_row = nc.dram_tensor("b_row", [1, 1536], F32, kind="ExternalInput").ap()
        kT_rows, nkv = 512, 4
    else:
        WB = 1296
        codesT = nc.dram_tensor("codesT_all", [768, H], I32, kind="ExternalInput").ap()
        scalesT = nc.dram_tensor("scalesT_all", [768, 32], F32, kind="ExternalInput").ap()
        wffnT = nc.dram_tensor("wffnT", [H, FC], F32, kind="ExternalInput").ap()
        bffn_row = nc.dram_tensor("bffn_row", [1, FC], F32, kind="ExternalInput").ap()
        loraA_arr = nc.dram_tensor("loraA_arr", [128, 32 * 48], F32, kind="ExternalInput").ap()
        loraB_arr = nc.dram_tensor("loraB_arr", [16, 3 * FC], F32, kind="ExternalInput").ap()
        lsi = nc.dram_tensor("lsi", [1, 4], I32, kind="ExternalInput").ap()
        cos_in = nc.dram_tensor("cos_arr", [128, 4096], F32, kind="ExternalInput").ap()
        sin_in = nc.dram_tensor("sin_arr", [128, 4096], F32, kind="ExternalInput").ap()
        kT_rows, nkv = 128, 1
    qT_o = nc.dram_tensor("qT", [512, S], BF16, kind="ExternalOutput").ap()
    kT_o = nc.dram_tensor("kT", [kT_rows, S], BF16, kind="ExternalOutput").ap()
    v_o = nc.dram_tensor("v4", [nkv, 64, 128, 128], BF16, kind="ExternalOutput").ap()
    if layer == 0:
        xred_o = nc.dram_tensor("xredT", [512, S], BF16, kind="ExternalOutput").ap()
    else:
        ffn_o = nc.dram_tensor("ffn1", [S, FC], BF16, kind="ExternalOutput").ap()

    NKT = 4 if layer == 0 else 1
    if layer == 0:
        n_stores = (4 + NKT) + 16 + 4
    else:
        n_stores = (4 + NKT) + 4 + 1
    CH = 16 if _PROJ_TRUNC is None else _PROJ_TRUNC

    with ExitStack() as st:
        sb = lambda n, s_, d: st.enter_context(nc.sbuf_tensor(n, s_, d))
        psum = lambda n, s_, d: st.enter_context(nc.psum_tensor(n, s_, d))
        s = _sems(nc, st, ["wl", "ident", "ct0", "ct1", "prep", "dq", "tpw", "tpc",
                           "xnld0", "xnld1", "mm", "mm1", "t1ev", "tt", "t1T", "qn",
                           "evt", "tp", "tpev", "store", "chunkdone", "sel", "vch"])
        ct_p = [s["ct0"], s["ct1"]]
        xnld_p = [s["xnld0"], s["xnld1"]]
        VCH = [0]
        def mkedge(v):
            def edge(ins):
                VCH[0] += 1
                ins.then_inc(s["vch"], 1)
                v.wait_ge(s["vch"], VCH[0])
                return ins
            return edge
        W_sb = sb("W_sb", [128, 32 * WB], BF16)
        ident = sb("ident_sb", [128, 128], BF16)
        ones_sb = sb("ones_sb", [1, 128], BF16)
        dummyp = sb("dummyp", [1, 8], F32)
        brow_sb = sb("brow_sb", [1, WB if layer == 0 else FC], BF16)
        if layer == 1:
            sc_sb = sb("sc_sb", [128, 6 * 32], F32)
            beff_sb = sb("beff_sb", [16, FC], BF16)
            selbc = sb("selbc", [128, 3], F32)
        psA = [psum(f"psA{i}", [128, 512], F32) for i in range(2)]
        psB = [psum(f"psB{i}", [128, 512], F32) for i in range(2)]
        psC = [psum(f"psC{i}", [128, 512], F32) for i in range(2)]
        tp = [psum(f"tp{i}", [128, 1024], BF16) for i in range(2)]
        block = st.enter_context(nc.Block())

        if layer == 0:
            @block.gpsimd
            def _(g):
                g.dma_start(ident[:], ident_in).then_inc(s["ident"], 16)
                g.dma_start(W_sb[:].rearrange("p (n f) -> p n f", f=WB),
                            w_all.rearrange("(n p) f -> p n f", p=128)
                            ).then_inc(s["wl"], 16)
                g.dma_start(brow_sb[:], b_row).then_inc(s["wl"], 16)
            WL_FINAL = 32

            @block.vector
            def _(v):
                v.memset(ones_sb[:], 1.0).then_inc(s["sel"], 1)
            SEL_FINAL = 1
            TPC_FINAL = 0
        else:
            with ExitStack() as stp:
                sbp = lambda n, s_, d: stp.enter_context(nc.sbuf_tensor(n, s_, d))
                ct_sb = [sbp(f"ct{i}", [128, H], I32) for i in range(2)]
                deq_sb = [sbp(f"dq{i}", [128, H], BF16) for i in range(2)]
                a_sb = sbp("a_sb", [128, 32 * 48], F32)
                b3_sb = sbp("b3_sb", [16, 3 * FC], F32)
                lsi_sb = sbp("lsi_sb", [1, 4], I32)
                lsif = sbp("lsif", [1, 4], F32)
                ge = sbp("ge", [1, 4], F32)
                tsc = sbp("tsc", [1, 8], F32)
                sel_sb = sbp("sel_sb", [1, 3], BF16)
                atmp = sbp("atmp", [128, 16], F32)
                atmp2 = sbp("atmp2", [128, 16], F32)

                @block.gpsimd
                def _(g):
                    g.dma_start(ident[:], ident_in).then_inc(s["ident"], 16)
                    g.dma_start(W_sb[:].rearrange("p (n f) -> p n f", f=WB)
                                [:, :, 784:1296],
                                wffnT.rearrange("(n p) f -> p n f", p=128)
                                ).then_inc(s["wl"], 16)
                    g.dma_start(brow_sb[:], bffn_row).then_inc(s["wl"], 16)
                    g.dma_start(sc_sb[:].rearrange("p (n f) -> p n f", f=32),
                                scalesT.rearrange("(n p) f -> p n f", p=128)
                                ).then_inc(s["wl"], 16)
                    g.dma_start(a_sb[:], loraA_arr).then_inc(s["wl"], 16)
                    g.dma_start(b3_sb[:], loraB_arr).then_inc(s["wl"], 16)
                    g.dma_start(lsi_sb[:], lsi).then_inc(s["wl"], 16)
                    for ft in range(6):
                        if ft >= 2:
                            g.wait_ge(s["prep"], ft - 1)
                        g.dma_start(ct_sb[ft % 2][:], codesT[ft * 128:(ft + 1) * 128, :]
                                    ).then_inc(ct_p[ft % 2], 16)
                WL_FINAL = 16 * 6

                @block.scalar
                def _(sc):
                    for ft in range(6):
                        sc.wait_ge(ct_p[ft % 2], 16 * (ft // 2 + 1))
                        if ft >= 2:
                            sc.wait_ge(s["tpw"], 32 * (ft - 1))
                        sc.activation(deq_sb[ft % 2][:], ct_sb[ft % 2][:], AF.Copy,
                                      bias=-8.0).then_inc(s["prep"], 1)

                @block.vector
                def _(v):
                    edge = mkedge(v)
                    v.wait_ge(s["wl"], WL_FINAL)
                    edge(v.tensor_copy(lsif[:], lsi_sb[:]))
                    edge(v.tensor_single_scalar(ge[:], lsif[:], 0.0, ALU.is_le))
                    edge(v.tensor_scalar(tsc[:, 0:1], ge[:, 3:4], 3.0, None, ALU.add))
                    edge(v.tensor_tensor(tsc[:, 1:2], ge[:, 2:3], tsc[:, 0:1], ALU.mult))
                    edge(v.tensor_scalar(tsc[:, 2:3], ge[:, 1:2], 2.0, None, ALU.mult))
                    edge(v.tensor_scalar(tsc[:, 3:4], ge[:, 1:2], -1.0, 1.0, ALU.mult,
                                         ALU.add))
                    edge(v.tensor_tensor(tsc[:, 4:5], tsc[:, 3:4], ge[:, 0:1], ALU.mult))
                    edge(v.tensor_tensor(tsc[:, 5:6], tsc[:, 2:3], tsc[:, 4:5], ALU.add))
                    edge(v.tensor_scalar(tsc[:, 6:7], ge[:, 2:3], -1.0, 1.0, ALU.mult,
                                         ALU.add))
                    edge(v.tensor_tensor(tsc[:, 7:8], tsc[:, 6:7], tsc[:, 5:6], ALU.mult))
                    edge(v.tensor_tensor(tsc[:, 0:1], tsc[:, 1:2], tsc[:, 7:8], ALU.add))
                    for j in range(3):
                        edge(v.tensor_single_scalar(sel_sb[:, j:j + 1], tsc[:, 0:1],
                                                    float(j + 2), ALU.is_equal))
                    v.memset(ones_sb[:], 1.0).then_inc(s["sel"], 1)

                @block.tensor
                def _(t):
                    t.wait_ge(s["sel"], 1)
                    t.matmul(psA[0][:, 0:3], ones_sb[:], sel_sb[:],
                             start=True, stop=True).then_inc(s["sel"], 1)

                @block.vector
                def _(v):
                    edge = mkedge(v)
                    v.wait_ge(s["sel"], 2)
                    edge(v.tensor_copy(selbc[:], psA[0][:, 0:3]))
                    v.memset(dummyp[0:1, :], 0.0).then_inc(s["sel"], 1)
                    edge(v.tensor_scalar(beff_sb[0:16, :], b3_sb[0:16, 0:FC],
                                         selbc[0:16, 0:1], None, ALU.mult))
                    edge(v.scalar_tensor_tensor(beff_sb[0:16, :], b3_sb[0:16, FC:2 * FC],
                                                selbc[0:16, 1:2], beff_sb[0:16, :],
                                                ALU.mult, ALU.add))
                    v.scalar_tensor_tensor(beff_sb[0:16, :], b3_sb[0:16, 2 * FC:3 * FC],
                                           selbc[0:16, 2:3], beff_sb[0:16, :],
                                           ALU.mult, ALU.add).then_inc(s["sel"], 1)
                    for ht in range(32):
                        edge(v.tensor_scalar(atmp[:], a_sb[:, ht * 48:ht * 48 + 16],
                                             selbc[:, 0:1], None, ALU.mult))
                        edge(v.scalar_tensor_tensor(atmp2[:],
                                                    a_sb[:, ht * 48 + 16:ht * 48 + 32],
                                                    selbc[:, 1:2], atmp[:],
                                                    ALU.mult, ALU.add))
                        edge(v.scalar_tensor_tensor(
                            W_sb[:, ht * WB + 768:ht * WB + 784],
                            a_sb[:, ht * 48 + 32:ht * 48 + 48], selbc[:, 2:3], atmp2[:],
                            ALU.mult, ALU.add))
                    v.memset(dummyp[0:1, :], 0.0).then_inc(s["sel"], 1)
                    nt = 0
                    for ft in range(6):
                        v.wait_ge(s["prep"], ft + 1)
                        for ht in range(32):
                            ins = v.tensor_scalar(
                                deq_sb[ft % 2][:, ht * 128:(ht + 1) * 128],
                                deq_sb[ft % 2][:, ht * 128:(ht + 1) * 128],
                                sc_sb[:, ft * 32 + ht:ft * 32 + ht + 1], None, ALU.mult)
                        ins.then_inc(s["dq"], 1)
                        dst0 = [0, 128, 256, 384, 512, 640][ft]
                        for ht in range(32):
                            v.wait_ge(s["tpw"], nt + 1)
                            v.tensor_copy(W_sb[:, ht * WB + dst0:ht * WB + dst0 + 128],
                                          tp[nt % 2][:, 0:128]).then_inc(s["tpc"], 1)
                            nt += 1

                @block.tensor
                def _(t):
                    t.wait_ge(s["ident"], 16)
                    nt = 0
                    for ft in range(6):
                        t.wait_ge(s["dq"], ft + 1)
                        for ht in range(32):
                            if nt >= 2:
                                t.wait_ge(s["tpc"], nt - 1)
                            t.transpose(tp[nt % 2][:, 0:128],
                                        deq_sb[ft % 2][:, ht * 128:(ht + 1) * 128],
                                        ident[:]).then_inc(s["tpw"], 1)
                            nt += 1
            SEL_FINAL = 5
            TPC_FINAL = 192

        xn_sb = [sb(f"xn{i}", [128, 32 * 512], BF16) for i in range(2)]
        qnat = [sb(f"qnat{i}", [128, 512], BF16) for i in range(2)]
        knat = [sb(f"knat{i}", [128, 128 * NKT], BF16) for i in range(2)]
        qkstage = sb("qks", [128, (4 + NKT) * 512], BF16)
        vstage = sb("vs", [128, 4 * (512 if layer == 0 else 128)], BF16)
        if layer == 0:
            xred = sb("xr", [128, 4 * 512], BF16)
            xrt = [sb(f"xrt{i}", [128, 512], BF16) for i in range(4)]
        else:
            cosc = [sb(f"cosc{i}", [128, 256], F32) for i in range(2)]
            sinc = [sb(f"sinc{i}", [128, 256], F32) for i in range(2)]
            qf32 = [sb(f"qf{i}", [128, 512], F32) for i in range(2)]
            kf32 = [sb(f"kf{i}", [128, 128], F32) for i in range(2)]
            rtmp1 = sb("rtmp1", [128, 64], F32)
            rtmp2 = sb("rtmp2", [128, 64], F32)
            ffnstage = sb("fs", [128, 4 * 512], BF16)
            t1_sb = [sb(f"t1_{i}", [128, 16], BF16) for i in range(2)]
            t1T_sb = [sb(f"t1T_{i}", [16, 128], BF16) for i in range(2)]

        def prep_gates(eng):
            eng.wait_ge(s["wl"], WL_FINAL)
            eng.wait_ge(s["ident"], 16)
            eng.wait_ge(s["sel"], SEL_FINAL)
            if layer == 1:
                eng.wait_ge(s["tpc"], TPC_FINAL)

        N_LD = 1 if layer == 0 else 3

        @block.sync
        def _(sync):
            prep_gates(sync)
            for cs in range(CH):
                if cs >= 2:
                    sync.wait_ge(s["chunkdone"], cs - 1)
                sync.dma_start(
                    xn_sb[cs % 2][:].rearrange("p (n f) -> p n f", f=512),
                    xnT[:, cs * 512:(cs + 1) * 512].rearrange("(n p) f -> p n f", p=128),
                ).then_inc(xnld_p[cs % 2], 16)
                if layer == 1:
                    sync.dma_start(cosc[cs % 2][:], cos_in[:, cs * 256:(cs + 1) * 256]
                                   ).then_inc(xnld_p[cs % 2], 16)
                    sync.dma_start(sinc[cs % 2][:], sin_in[:, cs * 256:(cs + 1) * 256]
                                   ).then_inc(xnld_p[cs % 2], 16)

        def emit_transposes(t, j):
            t.wait_ge(s["qn"], j + 1)
            if j >= 2:
                t.wait_ge(s["tpev"], j - 1)
            par = j % 2
            for ft in range(4):
                ins = t.transpose(tp[par][:, ft * 128:(ft + 1) * 128],
                                  qnat[par][:, ft * 128:(ft + 1) * 128], ident[:])
            for kt in range(NKT):
                ins = t.transpose(tp[par][:, (4 + kt) * 128:(5 + kt) * 128],
                                  knat[par][:, kt * 128:(kt + 1) * 128], ident[:])
            ins.then_inc(s["tp"], 1)

        @block.tensor
        def _(t):
            prep_gates(t)
            for cs in range(CH):
                for stx in range(4):
                    i = 4 * cs + stx
                    par = i % 2
                    if stx == 0:
                        t.wait_ge(xnld_p[cs % 2], N_LD * 16 * (cs // 2 + 1))
                    if i >= 2:
                        t.wait_ge(s["evt"], i - 1)
                    lhs = lambda ht: xn_sb[cs % 2][:, ht * 512 + stx * 128:
                                                   ht * 512 + stx * 128 + 128]
                    if layer == 0:
                        for ht in range(32):
                            t.matmul(psA[par][:], lhs(ht), W_sb[:, ht * WB:ht * WB + 512],
                                     start=(ht == 0), stop=False)
                            t.matmul(psB[par][:], lhs(ht),
                                     W_sb[:, ht * WB + 512:ht * WB + 1024],
                                     start=(ht == 0), stop=False)
                            t.matmul(psC[par][:], lhs(ht),
                                     W_sb[:, ht * WB + 1024:ht * WB + 1536],
                                     start=(ht == 0), stop=False)
                        t.matmul(psA[par][:], ones_sb[:], brow_sb[0:1, 0:512],
                                 start=False, stop=True)
                        t.matmul(psB[par][:], ones_sb[:], brow_sb[0:1, 512:1024],
                                 start=False, stop=True)
                        t.matmul(psC[par][:], ones_sb[:], brow_sb[0:1, 1024:1536],
                                 start=False, stop=True).then_inc(s["mm"], 1)
                    else:
                        for ht in range(32):
                            t.matmul(psA[par][:], lhs(ht), W_sb[:, ht * WB:ht * WB + 512],
                                     start=(ht == 0), stop=(ht == 31))
                            t.matmul(psB[par][:, 0:272], lhs(ht),
                                     W_sb[:, ht * WB + 512:ht * WB + 784],
                                     start=(ht == 0), stop=(ht == 31))
                            t.matmul(psC[par][:], lhs(ht),
                                     W_sb[:, ht * WB + 784:ht * WB + 1296],
                                     start=(ht == 0), stop=False)
                        t.matmul(psC[par][:], ones_sb[:], brow_sb[0:1, :],
                                 start=False, stop=False).then_inc(s["mm1"], 1)
                        t.wait_ge(s["t1ev"], i + 1)
                        if i >= 2:
                            t.wait_ge(s["tpev"], i - 1)
                        t.transpose(tp[par][0:16, 0:128], t1_sb[par][:],
                                    ident[:]).then_inc(s["tt"], 1)
                        t.wait_ge(s["t1T"], i + 1)
                        t.matmul(psC[par][:], t1T_sb[par][0:16, :], beff_sb[0:16, :],
                                 start=False, stop=True).then_inc(s["mm"], 1)
                    if i >= 1:
                        emit_transposes(t, i - 1)
            emit_transposes(t, CH * 4 - 1)

        @block.vector
        def _(v):
            edge = mkedge(v)
            for cs in range(CH):
                for stx in range(4):
                    i = 4 * cs + stx
                    par = i % 2
                    if layer == 1:
                        v.wait_ge(s["mm1"], i + 1)
                        v.tensor_copy(t1_sb[par][:], psB[par][:, 256:272]
                                      ).then_inc(s["t1ev"], 1)
                        v.wait_ge(s["tt"], i + 1)
                        v.tensor_copy(t1T_sb[par][0:16, :], tp[par][0:16, 0:128]
                                      ).then_inc(s["t1T"], 1)
                    if i >= 1:
                        j = i - 1
                        v.wait_ge(s["tp"], j + 1)
                        for ft in range(4 + NKT):
                            ins = v.tensor_copy(
                                qkstage[:, ft * 512 + (j % 4) * 128:
                                        ft * 512 + (j % 4) * 128 + 128],
                                tp[j % 2][:, ft * 128:(ft + 1) * 128])
                        ins.then_inc(s["tpev"], 1)
                    v.wait_ge(s["mm"], i + 1)
                    if stx == 0 and cs >= 1:
                        v.wait_ge(s["store"], n_stores * 16 * cs)
                    if layer == 0:
                        v.tensor_copy(qnat[par][:], psA[par][:])
                        v.tensor_copy(knat[par][:], psB[par][:]).then_inc(s["qn"], 1)
                        v.tensor_copy(vstage[:, stx * 512:(stx + 1) * 512], psC[par][:]
                                      ).then_inc(s["evt"], 1)
                    else:
                        if stx == 0:
                            v.wait_ge(xnld_p[cs % 2], N_LD * 16 * (cs // 2 + 1))
                        edge(v.tensor_copy(qf32[par][:], psA[par][:]))
                        edge(v.tensor_copy(kf32[par][:], psB[par][:, 0:128]))
                        v.tensor_copy(vstage[:, stx * 128:(stx + 1) * 128],
                                      psB[par][:, 128:256])
                        v.tensor_copy(ffnstage[:, stx * 512:(stx + 1) * 512], psC[par][:]
                                      ).then_inc(s["evt"], 1)
                        cosA = cosc[cs % 2][:, stx * 64:(stx + 1) * 64]
                        sinA = sinc[cs % 2][:, stx * 64:(stx + 1) * 64]
                        for hh in range(4):
                            qe = qf32[par][:, hh * 128:hh * 128 + 64]
                            qo = qf32[par][:, hh * 128 + 64:hh * 128 + 128]
                            v.tensor_tensor(rtmp1[:], qe, cosA, ALU.mult)
                            edge(v.tensor_tensor(rtmp2[:], qo, sinA, ALU.mult))
                            edge(v.tensor_tensor(qnat[par][:, hh * 128:hh * 128 + 64],
                                                 rtmp1[:], rtmp2[:], ALU.subtract))
                            v.tensor_tensor(rtmp1[:], qo, cosA, ALU.mult)
                            edge(v.tensor_tensor(rtmp2[:], qe, sinA, ALU.mult))
                            edge(v.tensor_tensor(qnat[par][:, hh * 128 + 64:
                                                           hh * 128 + 128],
                                                 rtmp1[:], rtmp2[:], ALU.add))
                        ke = kf32[par][:, 0:64]
                        ko = kf32[par][:, 64:128]
                        v.tensor_tensor(rtmp1[:], ke, cosA, ALU.mult)
                        edge(v.tensor_tensor(rtmp2[:], ko, sinA, ALU.mult))
                        edge(v.tensor_tensor(knat[par][:, 0:64], rtmp1[:], rtmp2[:],
                                             ALU.subtract))
                        v.tensor_tensor(rtmp1[:], ko, cosA, ALU.mult)
                        edge(v.tensor_tensor(rtmp2[:], ke, sinA, ALU.mult))
                        v.tensor_tensor(knat[par][:, 64:128], rtmp1[:], rtmp2[:],
                                        ALU.add).then_inc(s["qn"], 1)
                if layer == 0:
                    for rt in range(4):
                        blocks = [xn_sb[cs % 2][:, (4 * g + rt) * 512:
                                                (4 * g + rt + 1) * 512] for g in range(8)]
                        v.tensor_tensor(xrt[0][:], blocks[0], blocks[1], ALU.add)
                        v.tensor_tensor(xrt[1][:], blocks[2], blocks[3], ALU.add)
                        v.tensor_tensor(xrt[2][:], blocks[4], blocks[5], ALU.add)
                        edge(v.tensor_tensor(xrt[3][:], blocks[6], blocks[7], ALU.add))
                        v.tensor_tensor(xrt[0][:], xrt[0][:], xrt[1][:], ALU.add)
                        edge(v.tensor_tensor(xrt[2][:], xrt[2][:], xrt[3][:], ALU.add))
                        edge(v.tensor_tensor(xred[:, rt * 512:(rt + 1) * 512],
                                             xrt[0][:], xrt[2][:], ALU.add))
                v.memset(dummyp[0:1, :], 0.0).then_inc(s["chunkdone"], 1)
            jlast = CH * 4 - 1
            v.wait_ge(s["tp"], jlast + 1)
            for ft in range(4 + NKT):
                ins = v.tensor_copy(qkstage[:, ft * 512 + 3 * 128:ft * 512 + 3 * 128 + 128],
                                    tp[jlast % 2][:, ft * 128:(ft + 1) * 128])
            ins.then_inc(s["tpev"], 1)

        @block.gpsimd
        def _(g):
            for cs in range(CH):
                g.wait_ge(s["tpev"], 4 * (cs + 1))
                g.wait_ge(s["chunkdone"], cs + 1)
                for ft in range(4):
                    g.dma_start(qT_o[ft * 128:(ft + 1) * 128, cs * 512:(cs + 1) * 512],
                                qkstage[:, ft * 512:(ft + 1) * 512]).then_inc(s["store"], 16)
                for kt in range(NKT):
                    g.dma_start(kT_o[kt * 128:(kt + 1) * 128, cs * 512:(cs + 1) * 512],
                                qkstage[:, (4 + kt) * 512:(5 + kt) * 512]
                                ).then_inc(s["store"], 16)
                for stx in range(4):
                    jj = 4 * cs + stx
                    if layer == 0:
                        for hh in range(4):
                            g.dma_start(v_o[hh, jj],
                                        vstage[:, stx * 512 + hh * 128:
                                               stx * 512 + (hh + 1) * 128]
                                        ).then_inc(s["store"], 16)
                    else:
                        g.dma_start(v_o[0, jj], vstage[:, stx * 128:(stx + 1) * 128]
                                    ).then_inc(s["store"], 16)
                if layer == 0:
                    for rt in range(4):
                        g.dma_start(xred_o[rt * 128:(rt + 1) * 128, cs * 512:(cs + 1) * 512],
                                    xred[:, rt * 512:(rt + 1) * 512]).then_inc(s["store"], 16)
                else:
                    g.dma_start(ffn_o[cs * 512:(cs + 1) * 512, :].rearrange(
                        "(j p) f -> p j f", p=128),
                        ffnstage[:].rearrange("p (j f) -> p j f", f=512)
                    ).then_inc(s["store"], 16)
            g.wait_ge(s["store"], n_stores * 16 * CH)
    return nc



def build_attn(layer):
    nc = bass.Bass("TRN2", target_bir_lowering=False)
    slots = L0_SLOTS if layer == 0 else L1_SLOTS
    nkv = 4 if layer == 0 else 1
    qT = nc.dram_tensor("qT", [512, S], BF16, kind="ExternalInput").ap()
    kT_i = nc.dram_tensor("kT", [nkv * 128, S], BF16, kind="ExternalInput").ap()
    v_i = nc.dram_tensor("v4", [nkv, 64, 128, 128], BF16, kind="ExternalInput").ap()
    ident_in = nc.dram_tensor("ident", [128, 128], BF16, kind="ExternalInput").ap()
    qsubs_used = sorted({q for q, _ in slots})
    masks_in = {}
    for qsub in qsubs_used:
        nD = {64: 2, 256: 2, 512: 4}[qsub]
        masks_in[qsub] = nc.dram_tensor(f"mask{qsub}", [128, nD * qsub], F32,
                                        kind="ExternalInput").ap()
    if layer == 0:
        slope_in = nc.dram_tensor("slope_bc", [128, 4], F32, kind="ExternalInput").ap()
        kpos_in = nc.dram_tensor("kpos", [128, 64], F32, kind="ExternalInput").ap()
    else:
        mcol_in = nc.dram_tensor("mcol", [128, 1], F32, kind="ExternalInput").ap()
    ctxT_o = nc.dram_tensor("ctxT", [512, S], BF16, kind="ExternalOutput").ap()

    sched = []
    for sl, (qsub, win) in enumerate(slots):
        if _ATTN_TRUNC is not None and sl >= _ATTN_TRUNC[0]:
            continue
        qlist = []
        for qs in range(0, S, qsub):
            ks_hi = (qs + qsub - 1) // 128
            ks_lo = 0 if win is None else max(0, (qs - win) // 128)
            kl = [(ks, (qs - ks * 128) if (qs - ks * 128) < 128 else None)
                  for ks in range(ks_lo, ks_hi + 1)]
            qlist.append((qs, kl))
        if _ATTN_TRUNC is not None:
            qlist = qlist[:_ATTN_TRUNC[1]]
        sched.append((sl, qsub, win, qlist))
    mask_off = {64: {64: 0, 0: 1}, 256: {0: 0, -128: 1},
                512: {0: 0, -128: 1, -256: 2, -384: 3}}
    exp_at_qsub_end = []
    tot = 0
    for sl, qsub, win, qlist in sched:
        for qs, kl in qlist:
            tot += len(kl)
            exp_at_qsub_end.append(tot)

    with ExitStack() as st:
        sb = lambda n, s_, d: st.enter_context(nc.sbuf_tensor(n, s_, d))
        psum = lambda n, s_, d: st.enter_context(nc.psum_tensor(n, s_, d))
        nbuf = 2 if layer == 0 else 1
        kv_k = [sb(f"kvk{i}", [128, S], BF16) for i in range(nbuf)]
        kv_v = [sb(f"kvv{i}", [128, 64 * 132], BF16) for i in range(nbuf)]
        q_sb = [sb(f"qsb{i}", [128, S], BF16) for i in range(2)]
        probs = [sb(f"pr{i}", [128, 512], BF16) for i in range(2)]
        ctxnat = sb("ctxnat", [128, 512], BF16)
        ctstage = [sb(f"cts{i}", [128, 512], BF16) for i in range(2)]
        ident = sb("ident_sb", [128, 128], BF16)
        mask_sb = {}
        for qsub in qsubs_used:
            nD = {64: 2, 256: 2, 512: 4}[qsub]
            mask_sb[qsub] = sb(f"msk{qsub}", [128, nD * qsub], F32)
        if layer == 0:
            slope_sb = sb("slope_sb", [128, 4], F32)
            kpos_sb = sb("kpos_sb", [128, 64], F32)
            btab = sb("btab", [128, 64], F32)
            ctmp = sb("ctmp", [128, 1], F32)
        else:
            mcol_sb = sb("mcol_sb", [128, 1], F32)
        rc = sb("rc", [128, 1], F32)
        dummy = sb("dummy_a", [1, 8], F32)
        scps = [psum(f"scps{i}", [128, 512], F32) for i in range(2)]
        ctxps = [psum(f"ctxps{i}", [128, 512], F32) for i in range(4)]
        ctp = psum("ctp", [128, 1024], BF16)

        s = _sems(nc, st, ["const", "lds0", "lds1", "sc", "mk", "bt", "exp", "pb",
                           "cv", "ctp", "ctev", "out0", "out1", "slotdone", "ones",
                           "vch"])
        lds_p = [s["lds0"], s["lds1"]]
        out_p = [s["out0"], s["out1"]]
        block = st.enter_context(nc.Block())

        nslots = len(sched)
        n_const = 1 + len(qsubs_used) + (2 if layer == 0 else 1)
        ld_gate = []
        ptot = [0, 0]
        for sl in range(nslots):
            n = (66 if (layer == 0 or sl == 0) else 1)
            ptot[sl % 2] += 16 * n
            ld_gate.append((sl % 2, ptot[sl % 2]))

        @block.sync
        def _(sync):
            sync.dma_start(ident[:], ident_in).then_inc(s["const"], 16)
            for qsub in qsubs_used:
                sync.dma_start(mask_sb[qsub][:], masks_in[qsub]).then_inc(s["const"], 16)
            if layer == 0:
                sync.dma_start(slope_sb[:], slope_in).then_inc(s["const"], 16)
                sync.dma_start(kpos_sb[:], kpos_in).then_inc(s["const"], 16)
            else:
                sync.dma_start(mcol_sb[:], mcol_in).then_inc(s["const"], 16)
            for sl in range(nslots):
                if sl >= 2:
                    sync.wait_ge(s["slotdone"], sl - 1)
                par = sl % nbuf
                if layer == 0 or sl == 0:
                    sync.dma_start(kv_k[par][:], kT_i[sl * 128:(sl + 1) * 128, :]
                                   ).then_inc(lds_p[sl % 2], 16)
                    for j in range(64):
                        sync.dma_start(kv_v[par][:, j * 132:j * 132 + 128],
                                       v_i[sl if layer == 0 else 0, j]
                                       ).then_inc(lds_p[sl % 2], 16)
                sync.dma_start(q_sb[sl % 2][:], qT[sl * 128:(sl + 1) * 128, :]
                               ).then_inc(lds_p[sl % 2], 16)

        @block.tensor
        def _(t):
            ksc = 0
            nq = 0
            for sl, qsub, win, qlist in sched:
                par = sl % nbuf
                t.wait_ge(s["const"], 16 * n_const)
                t.wait_ge(lds_p[ld_gate[sl][0]], ld_gate[sl][1])
                t.wait_ge(s["ones"], min(sl + 1, nbuf) if layer == 0 else 1)
                nqt = (qsub + 127) // 128
                for qs, kl in qlist:
                    nv = len(kl)
                    for m, (ks, D) in enumerate(kl):
                        if ksc >= 2:
                            t.wait_ge(s["exp"], ksc - 1)
                        t.matmul(scps[ksc % 2][:, 0:qsub],
                                 kv_k[par][:, ks * 128:(ks + 1) * 128],
                                 q_sb[sl % 2][:, qs:qs + qsub],
                                 start=True, stop=True).then_inc(s["sc"], 1)
                        ksc += 1
                        if m >= 1:
                            t.wait_ge(s["exp"], ksc - 1)
                            pks = kl[m - 1][0]
                            for qt in range(nqt):
                                mp = min(128, qsub - qt * 128)
                                ins = t.matmul(
                                    ctxps[qt][0:mp, 0:129],
                                    probs[(ksc - 2) % 2][:, qt * 128:qt * 128 + mp],
                                    kv_v[par][:, pks * 132:pks * 132 + 129],
                                    start=(m == 1), stop=False)
                            ins.then_inc(s["pb"], 1)
                    t.wait_ge(s["exp"], ksc)
                    lks = kl[-1][0]
                    for qt in range(nqt):
                        mp = min(128, qsub - qt * 128)
                        ins = t.matmul(ctxps[qt][0:mp, 0:129],
                                       probs[(ksc - 1) % 2][:, qt * 128:qt * 128 + mp],
                                       kv_v[par][:, lks * 132:lks * 132 + 129],
                                       start=(nv == 1), stop=True)
                    ins.then_inc(s["pb"], 1)
                    nq += 1
                    t.wait_ge(s["cv"], nq)
                    if nq >= 2:
                        t.wait_ge(s["ctev"], nq - 1)
                    for qt in range(nqt):
                        ins = t.transpose(ctp[:, qt * 128:(qt + 1) * 128],
                                          ctxnat[:, qt * 128:(qt + 1) * 128], ident[:])
                    ins.then_inc(s["ctp"], 1)

        @block.vector
        def _(v):
            nch = [0]
            def edge(ins):
                nch[0] += 1
                ins.then_inc(s["vch"], 1)
                v.wait_ge(s["vch"], nch[0])
            ksc = 0
            nq = 0
            for sl, qsub, win, qlist in sched:
                par = sl % nbuf
                if (layer == 0 and sl < nbuf) or (layer == 1 and sl == 0):
                    v.wait_ge(s["const"], 16 * n_const)
                    v.wait_ge(lds_p[ld_gate[sl][0]], ld_gate[sl][1])
                    v.memset(kv_v[par][:].rearrange("p (j d) -> p j d", d=132)
                             [:, :, 128:132], 1.0).then_inc(s["ones"], 1)
                nqt = (qsub + 127) // 128
                for qs, kl in qlist:
                    if layer == 0:
                        if nq >= 1:
                            v.wait_ge(s["exp"], exp_at_qsub_end[nq - 1])
                        qref = float(qs + qsub - 1)
                        edge(v.tensor_scalar(ctmp[:], slope_sb[:, sl:sl + 1], -qref,
                                             -M_STAB, ALU.mult, ALU.add))
                        v.tensor_scalar(btab[:], kpos_sb[:], slope_sb[:, sl:sl + 1],
                                        ctmp[:], ALU.mult, ALU.add).then_inc(s["bt"], 1)
                    for m, (ks, D) in enumerate(kl):
                        if D is not None:
                            v.wait_ge(s["sc"], ksc + 1)
                            off = mask_off[qsub][D] * qsub
                            v.tensor_tensor(scps[ksc % 2][:, 0:qsub],
                                            scps[ksc % 2][:, 0:qsub],
                                            mask_sb[qsub][:, off:off + qsub],
                                            ALU.add).then_inc(s["mk"], 1)
                        ksc += 1
                    v.wait_ge(s["pb"], exp_at_qsub_end[nq])
                    for qt in range(nqt):
                        mp = min(128, qsub - qt * 128)
                        edge(v.reciprocal(rc[0:mp, :], ctxps[qt][0:mp, 128:129]))
                        edge(v.tensor_scalar(ctxnat[0:mp, qt * 128:(qt + 1) * 128],
                                             ctxps[qt][0:mp, 0:128], rc[0:mp, :],
                                             None, ALU.mult))
                    v.memset(dummy[0:1, :], 0.0).then_inc(s["cv"], 1)
                    v.wait_ge(s["ctp"], nq + 1)
                    if nq >= 2:
                        v.wait_ge(out_p[nq % 2], 16 * (nq // 2))
                    v.tensor_copy(ctstage[nq % 2][:, 0:qsub], ctp[:, 0:qsub]
                                  ).then_inc(s["ctev"], 1)
                    nq += 1
                v.memset(dummy[0:1, :], 0.0).then_inc(s["slotdone"], 1)

        @block.scalar
        def _(sc):
            ksc = 0
            nmk = 0
            nbt = 0
            for sl, qsub, win, qlist in sched:
                for qs, kl in qlist:
                    if layer == 0:
                        nbt += 1
                        sc.wait_ge(s["bt"], nbt)
                    for m, (ks, D) in enumerate(kl):
                        if D is not None:
                            nmk += 1
                            sc.wait_ge(s["mk"], nmk)
                        else:
                            sc.wait_ge(s["sc"], ksc + 1)
                        if ksc >= 2:
                            sc.wait_ge(s["pb"], ksc - 1)
                        bias = btab[:, ks:ks + 1] if layer == 0 else mcol_sb[:]
                        sc.activation(probs[ksc % 2][:, 0:qsub], scps[ksc % 2][:, 0:qsub],
                                      AF.Exp, bias=bias, scale=SCALE
                                      ).then_inc(s["exp"], 1)
                        ksc += 1

        @block.gpsimd
        def _(g):
            nq = 0
            for sl, qsub, win, qlist in sched:
                for qs, kl in qlist:
                    g.wait_ge(s["ctev"], nq + 1)
                    g.dma_start(ctxT_o[sl * 128:(sl + 1) * 128, qs:qs + qsub],
                                ctstage[nq % 2][:, 0:qsub]).then_inc(out_p[nq % 2], 16)
                    nq += 1
            g.wait_ge(out_p[0], 16 * ((nq + 1) // 2))
            g.wait_ge(out_p[1], 16 * (nq // 2))
    return nc



def build_out(layer):
    nc = bass.Bass("TRN2", target_bir_lowering=False)
    ctxT = nc.dram_tensor("ctxT", [H, S], BF16, kind="ExternalInput").ap()
    x_in = nc.dram_tensor("x_cols", [S, FC], F32, kind="ExternalInput").ap()
    ident_in = nc.dram_tensor("ident", [128, 128], BF16, kind="ExternalInput").ap()
    if layer == 0:
        w_in = nc.dram_tensor("woutp", [H, FC], F32, kind="ExternalInput").ap()
        b_in = nc.dram_tensor("b_row", [1, FC], F32, kind="ExternalInput").ap()
        kmat_in = nc.dram_tensor("kmat", [512, FC], I8, kind="ExternalInput").ap()
        xred_in = nc.dram_tensor("xredT", [512, S], BF16, kind="ExternalInput").ap()
    else:
        oc_in = nc.dram_tensor("o_codesT", [FC, H], I32, kind="ExternalInput").ap()
        os_in = nc.dram_tensor("o_scalesT", [FC, 32], F32, kind="ExternalInput").ap()
        ffn_in = nc.dram_tensor("ffn1", [S, FC], BF16, kind="ExternalInput").ap()
    xout = nc.dram_tensor("xout", [S, FC], F32, kind="ExternalOutput").ap()
    CH = 16

    with ExitStack() as st:
        sb = lambda n, s_, d: st.enter_context(nc.sbuf_tensor(n, s_, d))
        psum = lambda n, s_, d: st.enter_context(nc.psum_tensor(n, s_, d))
        s = _sems(nc, st, ["wl", "ident", "ct0", "ct1", "prep", "dq", "tpw", "tpc",
                           "ld0", "ld1", "mm", "ev", "store0", "store1", "chunkdone",
                           "ones", "vch"])
        ct_p = [s["ct0"], s["ct1"]]
        ld_p = [s["ld0"], s["ld1"]]
        store_p = [s["store0"], s["store1"]]
        W_sb = sb("W_sb", [128, 32 * 512], BF16)
        ident = sb("ident_sb", [128, 128], BF16)
        ones_sb = sb("ones_sb", [1, 128], BF16)
        dummyo = sb("dummyo", [1, 8], F32)
        psO = [psum(f"psO{i}", [128, 512], F32) for i in range(2)]
        tp = [psum(f"tp{i}", [128, 1024], BF16) for i in range(2)]
        block = st.enter_context(nc.Block())

        if layer == 0:
            ksb8 = sb("ksb8", [128, 4 * 512], I8)
            ksb = sb("ksb", [128, 4 * 512], BF16)
            brow_sb = sb("brow_sb", [1, FC], BF16)

            @block.gpsimd
            def _(g):
                g.dma_start(ident[:], ident_in).then_inc(s["ident"], 16)
                g.dma_start(W_sb[:].rearrange("p (n f) -> p n f", f=512),
                            w_in.rearrange("(n p) f -> p n f", p=128)
                            ).then_inc(s["wl"], 16)
                g.dma_start(brow_sb[:], b_in).then_inc(s["wl"], 16)
                g.dma_start(ksb8[:].rearrange("p (n f) -> p n f", f=512),
                            kmat_in.rearrange("(n p) f -> p n f", p=128)
                            ).then_inc(s["wl"], 16)
            WL_FINAL = 48

            @block.scalar
            def _(sc):
                sc.wait_ge(s["wl"], WL_FINAL)
                sc.activation(ksb[:], ksb8[:], AF.Copy).then_inc(s["prep"], 1)
            PREP_FINAL = 1
            TPC_FINAL = 0
        else:
            with ExitStack() as stp:
                sbp = lambda n, s_, d: stp.enter_context(nc.sbuf_tensor(n, s_, d))
                oc_sb = [sbp(f"oc{i}", [128, H], I32) for i in range(2)]
                deq_sb = [sbp(f"dq{i}", [128, H], BF16) for i in range(2)]
                osc_sb = sbp("osc_sb", [128, 4 * 32], F32)

                @block.gpsimd
                def _(g):
                    g.dma_start(ident[:], ident_in).then_inc(s["ident"], 16)
                    g.dma_start(osc_sb[:].rearrange("p (n f) -> p n f", f=32),
                                os_in.rearrange("(n p) f -> p n f", p=128)
                                ).then_inc(s["wl"], 16)
                    for ft in range(4):
                        if ft >= 2:
                            g.wait_ge(s["prep"], ft - 1)
                        g.dma_start(oc_sb[ft % 2][:], oc_in[ft * 128:(ft + 1) * 128, :]
                                    ).then_inc(ct_p[ft % 2], 16)
                WL_FINAL = 16

                @block.scalar
                def _(sc):
                    for ft in range(4):
                        sc.wait_ge(ct_p[ft % 2], 16 * (ft // 2 + 1))
                        if ft >= 2:
                            sc.wait_ge(s["tpw"], 32 * (ft - 1))
                        sc.activation(deq_sb[ft % 2][:], oc_sb[ft % 2][:], AF.Copy,
                                      bias=-8.0).then_inc(s["prep"], 1)
                PREP_FINAL = 4

                @block.vector
                def _(v):
                    nt = 0
                    v.wait_ge(s["wl"], 16)
                    for ft in range(4):
                        v.wait_ge(s["prep"], ft + 1)
                        for ht in range(32):
                            ins = v.tensor_scalar(
                                deq_sb[ft % 2][:, ht * 128:(ht + 1) * 128],
                                deq_sb[ft % 2][:, ht * 128:(ht + 1) * 128],
                                osc_sb[:, ft * 32 + ht:ft * 32 + ht + 1], None, ALU.mult)
                        ins.then_inc(s["dq"], 1)
                        for ht in range(32):
                            v.wait_ge(s["tpw"], nt + 1)
                            v.tensor_copy(
                                W_sb[:, ht * 512 + ft * 128:ht * 512 + (ft + 1) * 128],
                                tp[nt % 2][:, 0:128]).then_inc(s["tpc"], 1)
                            nt += 1

                @block.tensor
                def _(t):
                    t.wait_ge(s["ident"], 16)
                    nt = 0
                    for ft in range(4):
                        t.wait_ge(s["dq"], ft + 1)
                        for ht in range(32):
                            if nt >= 2:
                                t.wait_ge(s["tpc"], nt - 1)
                            t.transpose(tp[nt % 2][:, 0:128],
                                        deq_sb[ft % 2][:, ht * 128:(ht + 1) * 128],
                                        ident[:]).then_inc(s["tpw"], 1)
                            nt += 1
            TPC_FINAL = 128

        ctx_sb = [sb(f"cx{i}", [128, 32 * 512], BF16) for i in range(2)]
        x_sb = [sb(f"x{i}", [128, 4 * 512], F32) for i in range(2)]
        out_sb = [sb(f"o{i}", [128, 4 * 512], F32) for i in range(2)]
        if layer == 0:
            xr_sb = [sb(f"xr{i}", [128, 4 * 512], BF16) for i in range(2)]
        else:
            ffn_sb = [sb(f"fn{i}", [128, 4 * 512], BF16) for i in range(2)]

        @block.vector
        def _(v):
            v.memset(ones_sb[:], 1.0).then_inc(s["ones"], 1)

        def gates(e):
            e.wait_ge(s["wl"], WL_FINAL)
            e.wait_ge(s["ident"], 16)
            e.wait_ge(s["prep"], PREP_FINAL)
            e.wait_ge(s["ones"], 1)
            if layer == 1:
                e.wait_ge(s["tpc"], TPC_FINAL)

        N_LD = 3

        @block.sync
        def _(sync):
            gates(sync)
            for cs in range(CH):
                if cs >= 2:
                    sync.wait_ge(s["chunkdone"], cs - 1)
                sync.dma_start(ctx_sb[cs % 2][:].rearrange("p (n f) -> p n f", f=512),
                               ctxT[:, cs * 512:(cs + 1) * 512].rearrange(
                                   "(n p) f -> p n f", p=128)).then_inc(ld_p[cs % 2], 16)
                sync.dma_start(x_sb[cs % 2][:].rearrange("p (n f) -> p n f", f=512),
                               x_in[cs * 512:(cs + 1) * 512, :].rearrange(
                                   "(j p) f -> p j f", p=128)).then_inc(ld_p[cs % 2], 16)
                if layer == 0:
                    sync.dma_start(xr_sb[cs % 2][:].rearrange("p (n f) -> p n f", f=512),
                                   xred_in[:, cs * 512:(cs + 1) * 512].rearrange(
                                       "(n p) f -> p n f", p=128)).then_inc(ld_p[cs % 2], 16)
                else:
                    sync.dma_start(ffn_sb[cs % 2][:].rearrange("p (n f) -> p n f", f=512),
                                   ffn_in[cs * 512:(cs + 1) * 512, :].rearrange(
                                       "(j p) f -> p j f", p=128)).then_inc(ld_p[cs % 2], 16)

        @block.tensor
        def _(t):
            gates(t)
            for cs in range(CH):
                t.wait_ge(ld_p[cs % 2], N_LD * 16 * (cs // 2 + 1))
                for stx in range(4):
                    i = 4 * cs + stx
                    par = i % 2
                    if i >= 2:
                        t.wait_ge(s["ev"], i - 1)
                    for ht in range(32):
                        ins = t.matmul(psO[par][:],
                                       ctx_sb[cs % 2][:, ht * 512 + stx * 128:
                                                      ht * 512 + stx * 128 + 128],
                                       W_sb[:, ht * 512:(ht + 1) * 512],
                                       start=(ht == 0), stop=(layer == 1 and ht == 31))
                    if layer == 0:
                        for rt in range(4):
                            t.matmul(psO[par][:],
                                     xr_sb[cs % 2][:, rt * 512 + stx * 128:
                                                   rt * 512 + stx * 128 + 128],
                                     ksb[:, rt * 512:(rt + 1) * 512],
                                     start=False, stop=False)
                        t.matmul(psO[par][:], ones_sb[:], brow_sb[:],
                                 start=False, stop=True).then_inc(s["mm"], 1)
                    else:
                        ins.then_inc(s["mm"], 1)

        @block.vector
        def _(v):
            nch = [0]
            for cs in range(CH):
                for stx in range(4):
                    i = 4 * cs + stx
                    par = i % 2
                    v.wait_ge(s["mm"], i + 1)
                    if cs >= 2 and stx == 0:
                        v.wait_ge(store_p[cs % 2], 16 * (cs // 2))
                    ins = v.tensor_tensor(out_sb[cs % 2][:, stx * 512:(stx + 1) * 512],
                                          psO[par][:],
                                          x_sb[cs % 2][:, stx * 512:(stx + 1) * 512],
                                          ALU.add)
                    if layer == 1:
                        nch[0] += 1
                        ins.then_inc(s["vch"], 1)
                        v.wait_ge(s["vch"], nch[0])
                        v.tensor_tensor(out_sb[cs % 2][:, stx * 512:(stx + 1) * 512],
                                        out_sb[cs % 2][:, stx * 512:(stx + 1) * 512],
                                        ffn_sb[cs % 2][:, stx * 512:(stx + 1) * 512],
                                        ALU.add).then_inc(s["ev"], 1)
                    else:
                        ins.then_inc(s["ev"], 1)
                v.memset(dummyo[0:1, :], 0.0).then_inc(s["chunkdone"], 1)

        @block.gpsimd
        def _(g):
            for cs in range(CH):
                g.wait_ge(s["ev"], 4 * (cs + 1))
                g.dma_start(xout[cs * 512:(cs + 1) * 512, :].rearrange(
                    "(j p) f -> p j f", p=128),
                    out_sb[cs % 2][:].rearrange("p (j f) -> p j f", f=512)
                ).then_inc(store_p[cs % 2], 16)
            g.wait_ge(store_p[0], 16 * ((CH + 1) // 2))
            g.wait_ge(store_p[1], 16 * (CH // 2))
    return nc



_CACHE = {}


def _get(name, fn):
    if name not in _CACHE:
        _CACHE[name] = fn()
    return _CACHE[name]


def _run(nc, in_maps):
    return run_bass_kernel_spmd(nc, in_maps, core_ids=list(range(NC)))


def _masks_np(qsub):
    Ds = {64: [64, 0], 256: [0, -128], 512: [0, -128, -256, -384]}[qsub]
    m = np.zeros((128, len(Ds) * qsub), np.float32)
    for vi, D in enumerate(Ds):
        i = np.arange(128)[:, None]
        j = np.arange(qsub)[None, :]
        m[:, vi * qsub:(vi + 1) * qsub] = np.where(i > j + D, NEG, 0.0)
    return m


def _parity_perm():
    return np.concatenate([np.arange(0, 128, 2), np.arange(1, 128, 2)])


def kernel(**inp):
    f32 = np.float32
    x0 = np.ascontiguousarray(np.asarray(inp["L0_attn_hidden_states"],
                                         f32).reshape(S, H))
    ident = np.eye(128, dtype=BF)
    slopes = _alibi_slopes(32)
    kpos = (np.arange(128)[:, None] + 128.0 * np.arange(64)[None, :]).astype(f32)
    msk = {q: _masks_np(q) for q in (64, 256, 512)}

    epscol = np.full((128, 1), 1e-5, f32)
    nc_ln = _get("ln", build_ln)
    res = _run(nc_ln, [{"x": np.ascontiguousarray(x0[c * ROWS:(c + 1) * ROWS]),
                        "epscol": epscol} for c in range(NC)])
    xn0 = np.concatenate([res.results[c]["xn"] for c in range(NC)], axis=0)
    xn0T = np.ascontiguousarray(xn0.T)

    wqkv = np.asarray(inp["L0_attn_w_qkv"], f32)
    bqkv = np.asarray(inp["L0_attn_b_qkv"], f32)
    in_maps = []
    for c in range(NC):
        heads = [c, c + 8, c + 16, c + 24]
        qcols = np.concatenate([np.arange(h * 128, (h + 1) * 128) for h in heads])
        w_all = np.concatenate([wqkv[:, qcols], wqkv[:, H + qcols],
                                wqkv[:, 2 * H + qcols]], axis=1)
        b_row = np.concatenate([bqkv[qcols], bqkv[H + qcols], bqkv[2 * H + qcols]])[None, :]
        in_maps.append({"xnT": xn0T, "w_all": np.ascontiguousarray(w_all),
                        "b_row": np.ascontiguousarray(b_row), "ident": ident})
    res = _run(_get("p0", lambda: build_proj(0)), in_maps)
    p0 = res.results

    in_maps = []
    for c in range(NC):
        heads = [c, c + 8, c + 16, c + 24]
        slope_bc = np.ascontiguousarray(
            np.broadcast_to(slopes[heads][None, :], (128, 4)).astype(f32))
        in_maps.append({"qT": p0[c]["qT"], "kT": p0[c]["kT"], "v4": p0[c]["v4"],
                        "ident": ident, "mask64": msk[64], "mask256": msk[256],
                        "mask512": msk[512], "slope_bc": slope_bc, "kpos": kpos})
    res = _run(_get("a0", lambda: build_attn(0)), in_maps)
    ctxT0 = np.concatenate([res.results[c]["ctxT"] for c in range(NC)], axis=0)

    wout = np.asarray(inp["L0_attn_w_out"], f32)
    bout = np.asarray(inp["L0_attn_b_out"], f32)
    kern = np.ascontiguousarray(np.asarray(inp["L0_ffn_kernel"]).reshape(512, H))
    perm = np.zeros(32, np.int64)
    for cc in range(NC):
        for j in range(4):
            perm[4 * cc + j] = cc + 8 * j
    in_maps = []
    for c in range(NC):
        cols = np.arange(c * FC, (c + 1) * FC)
        woutp = np.concatenate([wout[p * 128:(p + 1) * 128, cols] for p in perm], axis=0)
        in_maps.append({"ctxT": ctxT0, "x_cols": np.ascontiguousarray(x0[:, cols]),
                        "woutp": np.ascontiguousarray(woutp),
                        "b_row": np.ascontiguousarray(bout[cols][None, :]),
                        "kmat": np.ascontiguousarray(kern[:, cols]),
                        "xredT": p0[c]["xredT"], "ident": ident})
    res = _run(_get("o0", lambda: build_out(0)), in_maps)
    x1 = np.concatenate([res.results[c]["xout"] for c in range(NC)], axis=1)

    res = _run(nc_ln, [{"x": np.ascontiguousarray(x1[c * ROWS:(c + 1) * ROWS]),
                        "epscol": epscol} for c in range(NC)])
    xn1 = np.concatenate([res.results[c]["xn"] for c in range(NC)], axis=0)
    xn1T = np.ascontiguousarray(xn1.T)

    pperm = _parity_perm()
    qc = np.asarray(inp["L1_attn_q_codes"])
    qs_ = np.asarray(inp["L1_attn_q_scales"], f32)
    kc_ = np.asarray(inp["L1_attn_k_codes"])
    ks_ = np.asarray(inp["L1_attn_k_scales"], f32)
    vc = np.asarray(inp["L1_attn_v_codes"])
    vs_ = np.asarray(inp["L1_attn_v_scales"], f32)
    wffn = np.asarray(inp["L1_ffn_weight"], f32)
    bffn = np.asarray(inp["L1_ffn_bias"], f32)
    lA = np.asarray(inp["L1_ffn_lora_A"], f32)
    lB = np.asarray(inp["L1_ffn_lora_B"], f32)
    lsi = np.ascontiguousarray(
        np.asarray(inp["L1_ffn_lora_start_indices"]).astype(np.int32).reshape(1, 4))
    pos = np.arange(S, dtype=np.float64)
    invf = 1.0 / (10000.0 ** (np.arange(0, HD, 2, dtype=np.float64) / HD))
    ang = pos[:, None] * invf[None, :]
    cos_arr = np.ascontiguousarray(np.cos(ang).astype(f32).reshape(64, 128, 64)
                                   .transpose(1, 0, 2).reshape(128, 4096))
    sin_arr = np.ascontiguousarray(np.sin(ang).astype(f32).reshape(64, 128, 64)
                                   .transpose(1, 0, 2).reshape(128, 4096))
    lAarr = np.ascontiguousarray(
        lA.transpose(2, 0, 1).reshape(32, 128, 3, 16).transpose(1, 0, 2, 3)
        .reshape(128, 32 * 48))

    in_maps = []
    for c in range(NC):
        qcols = np.concatenate([(4 * c + hh) * 128 + pperm for hh in range(4)])
        kcols = c * 128 + pperm
        vcols = np.arange(c * 128, (c + 1) * 128)
        codesT = np.ascontiguousarray(np.concatenate(
            [qc[:, qcols].T, kc_[:, kcols].T, vc[:, vcols].T], axis=0)).astype(np.int32)
        scalesT = np.ascontiguousarray(np.concatenate(
            [qs_[:, qcols].T, ks_[:, kcols].T, vs_[:, vcols].T], axis=0)).astype(f32)
        cols = np.arange(c * FC, (c + 1) * FC)
        wffnT = np.ascontiguousarray(wffn[cols, :].T)
        lBarr = np.ascontiguousarray(
            lB[:, cols, :].transpose(0, 2, 1).transpose(1, 0, 2).reshape(16, 3 * FC))
        in_maps.append({"xnT": xn1T, "codesT_all": codesT, "scalesT_all": scalesT,
                        "wffnT": wffnT,
                        "bffn_row": np.ascontiguousarray(bffn[cols][None, :]),
                        "loraA_arr": lAarr, "loraB_arr": lBarr, "lsi": lsi,
                        "cos_arr": cos_arr, "sin_arr": sin_arr, "ident": ident})
    res = _run(_get("p1", lambda: build_proj(1)), in_maps)
    p1 = res.results

    mcol = np.full((128, 1), -M_STAB, f32)
    in_maps = []
    for c in range(NC):
        in_maps.append({"qT": p1[c]["qT"], "kT": p1[c]["kT"], "v4": p1[c]["v4"],
                        "ident": ident, "mask512": msk[512], "mcol": mcol})
    res = _run(_get("a1", lambda: build_attn(1)), in_maps)
    ctxT1 = np.concatenate([res.results[c]["ctxT"] for c in range(NC)], axis=0)

    oc = np.asarray(inp["L1_attn_o_codes"])
    osc = np.asarray(inp["L1_attn_o_scales"], f32)
    in_maps = []
    for c in range(NC):
        cols = np.arange(c * FC, (c + 1) * FC)
        in_maps.append({"ctxT": ctxT1, "x_cols": np.ascontiguousarray(x1[:, cols]),
                        "o_codesT": np.ascontiguousarray(oc[:, cols].T).astype(np.int32),
                        "o_scalesT": np.ascontiguousarray(osc[:, cols].T).astype(f32),
                        "ffn1": p1[c]["ffn1"], "ident": ident})
    res = _run(_get("o1", lambda: build_out(1)), in_maps)
    x2 = np.concatenate([res.results[c]["xout"] for c in range(NC)], axis=1)
    return np.ascontiguousarray(x2.reshape(1, S, H)).astype(np.float32)
